# version 07 (final)
# baseline (speedup 1.0000x reference)
import os
import numpy as np
from contextlib import ExitStack

import concourse.bass as bass
import concourse.mybir as mybir
from concourse.bass import IndirectOffsetOnAxis
from concourse.bass_utils import run_bass_kernel_spmd

F32 = mybir.dt.float32
BF16 = mybir.dt.bfloat16
I32 = mybir.dt.int32
AF = mybir.ActivationFunctionType
ALU = mybir.AluOpType
AX = mybir.AxisListType

NCORES = 8
D = 1024
TLOC = 2048
T = 16384
H = 16
DH = 64
SCALE = DH ** -0.5
E = 8
C = 4096
FF = 4096
NCH = TLOC // 128
KD = D // 128
KF = FF // 128

DEBUG = False


class Sched:
    ENGS = ("sync", "gpsimd", "scalar", "vector", "tensor")

    def __init__(self):
        self.recs = {k: [] for k in self.ENGS}
        self.sems = {}

    def sem(self, es, nc, name):
        h = es.enter_context(nc.semaphore(name))
        self.sems[name] = [h, 0]
        return name

    def val(self, name):
        return self.sems[name][1]

    def op(self, eng, fn, waits=(), inc=None):
        w = [(self.sems[s][0], int(v)) for (s, v) in waits]
        t = None
        if inc is not None:
            s, amt = inc
            self.sems[s][1] += amt
            t = (self.sems[s][0], amt)
        self.recs[eng].append((w, fn, t))
        return self.sems[inc[0]][1] if inc else None

    def mark(self, eng, semname, amt=1):
        w, fn, t = self.recs[eng][-1]
        assert t is None, "last record already has an inc"
        self.sems[semname][1] += amt
        self.recs[eng][-1] = (w, fn, (self.sems[semname][0], amt))
        return self.sems[semname][1]

    def replay(self, eng_name, e):
        for w, fn, t in self.recs[eng_name]:
            for h, v in w:
                e.wait_ge(h, v)
            ins = fn(e)
            if t is not None:
                ins.then_inc(t[0], t[1])


def build_nc(debug=False):
    nc = bass.Bass("TRN2", target_bir_lowering=False, detect_race_conditions=False)

    x_my = nc.dram_tensor("x_my", [TLOC, D], F32, kind="ExternalInput").ap()
    Wq = nc.dram_tensor("Wq", [D, D], F32, kind="ExternalInput").ap()
    Wk = nc.dram_tensor("Wk", [D, D], F32, kind="ExternalInput").ap()
    Wv = nc.dram_tensor("Wv", [D, D], F32, kind="ExternalInput").ap()
    Wg = nc.dram_tensor("Wg", [D, D], F32, kind="ExternalInput").ap()
    Wo = nc.dram_tensor("Wo", [D, D], F32, kind="ExternalInput").ap()
    bgd = nc.dram_tensor("bgd", [1, D], F32, kind="ExternalInput").ap()
    wgrd = nc.dram_tensor("wgrd", [D, E], F32, kind="ExternalInput").ap()
    wegd = nc.dram_tensor("wegd", [D, FF], F32, kind="ExternalInput").ap()
    wedd = nc.dram_tensor("wedd", [FF, D], F32, kind="ExternalInput").ap()
    identFd = nc.dram_tensor("identFd", [128, 128], F32, kind="ExternalInput").ap()
    identBd = nc.dram_tensor("identBd", [128, 128], F32, kind="ExternalInput").ap()
    triBd = nc.dram_tensor("triBd", [128, 128], F32, kind="ExternalInput").ap()
    sel16d = nc.dram_tensor("sel16d", [128, 16], F32, kind="ExternalInput").ap()
    eidd = nc.dram_tensor("eidd", [128, 1], F32, kind="ExternalInput").ap()
    out_my = nc.dram_tensor("out_my", [TLOC, D], F32, kind="ExternalOutput").ap()

    dbg = {}
    if debug:
        for nm, shp, dt in (("x1", [TLOC, D], F32), ("lgt", [128, 128 * E], F32),
                            ("loc1s", [128, 128], F32), ("loc2s", [128, 128], F32),
                            ("metaT", [128, 128], F32), ("sidx", [128, 32], I32),
                            ("eo", [C, D], F32), ("idx1", [128, 128], F32),
                            ("g1n", [128, 128], F32), ("xnT", [128, KD * TLOC], BF16),
                            ("qT", [128, KD * TLOC], BF16), ("kvr", [128, KD * 128], BF16),
                            ("gbf", [128, NCH * D], BF16), ("att0", [128, D], BF16),
                            ("kbf0", [128, D], BF16), ("vbf0", [128, D], BF16),
                            ("obuf0", [128, D], F32), ("dzs0", [128, 16], F32),
                            ("zsb0", [128, 16], F32), ("ksbd", [128, 16], BF16)):
            dbg[nm] = nc.dram_tensor("dbg_" + nm, shp, dt, kind="ExternalOutput").ap()

    agt_in = nc.dram_tensor("agt_in", [TLOC, D], BF16).ap()
    agt_out = nc.dram_tensor("agt_out", [T, D], BF16).ap()
    agl_in = nc.dram_tensor("agl_in", [TLOC, E], F32).ap()
    agl_out = nc.dram_tensor("agl_out", [T, E], F32).ap()
    kv_din = nc.dram_tensor("kv_din", [65, D], F32).ap()
    kv_dout = nc.dram_tensor("kv_dout", [65, D], F32).ap()
    dispd = nc.dram_tensor("dispd", [C, D], BF16).ap()
    eo_din = nc.dram_tensor("eo_din", [C, D], F32).ap()
    eo_dout = nc.dram_tensor("eo_dout", [E * C, D], F32).ap()
    x1d = nc.dram_tensor("x1d", [TLOC, D], F32).ap()

    es = ExitStack()
    S = Sched()
    sb = lambda name, shape, dt: es.enter_context(nc.sbuf_tensor(name, shape, dt))

    with es:
        SC = S.sem(es, nc, "SC")
        VE = S.sem(es, nc, "VE")
        TE = S.sem(es, nc, "TE")
        TT = S.sem(es, nc, "TT")
        LDX = S.sem(es, nc, "LDX")
        XC = [S.sem(es, nc, f"XC{i}") for i in range(3)]
        X1S = [S.sem(es, nc, f"X1S{i}") for i in range(2)]
        DT = [S.sem(es, nc, f"DT{i}") for i in range(2)]
        RLG = S.sem(es, nc, "RLG")
        LDW = S.sem(es, nc, "LDW")
        WS0 = S.sem(es, nc, "WS0")
        WS1 = S.sem(es, nc, "WS1")
        KVL = S.sem(es, nc, "KVL")
        WEG = S.sem(es, nc, "WEG")
        WED = S.sem(es, nc, "WED")
        AG = S.sem(es, nc, "AG")
        IG = S.sem(es, nc, "IG")
        IGS = [S.sem(es, nc, f"IGS{i}") for i in range(2)]
        EG1S = [S.sem(es, nc, f"EG1S{i}") for i in range(2)]
        EG2S = [S.sem(es, nc, f"EG2S{i}") for i in range(2)]
        STKV = S.sem(es, nc, "STKV")
        STB = S.sem(es, nc, "STB")
        STX1 = S.sem(es, nc, "STX1")
        STLG = S.sem(es, nc, "STLG")
        STG = [S.sem(es, nc, f"STG{i}") for i in range(2)]
        STE = S.sem(es, nc, "STE")
        STO = [S.sem(es, nc, f"STO{i}") for i in range(2)]
        DBG = S.sem(es, nc, "DBG")

        idF = sb("idF", [128, 128], F32)
        idB = sb("idB", [128, 128], BF16)
        triB = sb("triB", [128, 128], BF16)
        sel16 = sb("sel16", [128, 16], F32)
        eid = sb("eid", [128, 1], F32)
        onesB = sb("onesB", [128, 128], BF16)
        onesF = sb("onesF", [128, 128], F32)
        iotaE = sb("iotaE", [128, 128 * E], F32)
        iotaF = sb("iotaF", [128, 128], F32)
        fp1 = sb("fp1", [128, 128], F32)
        iotaCH = sb("iotaCH", [128, 32], F32)
        iotaCH32 = sb("iotaCH32", [128, 32], F32)
        pcolF = sb("pcolF", [128, 1], F32)

        wgr_sb = sb("wgr_sb", [128, KD * E], F32)
        metaT_sb = sb("metaT_sb", [128, 128], F32)
        FI = sb("FI", [128, 32], I32)
        SIDX = sb("SIDX", [128, 32], I32)
        x1c = [sb(f"x1c{i}", [128, D], F32) for i in range(2)]
        xl2f = [sb(f"xl2f{i}", [128, D], F32) for i in range(2)]

        att_scope = ExitStack()
        asb = lambda name, shape, dt: att_scope.enter_context(nc.sbuf_tensor(name, shape, dt))
        xnT = asb("xnT", [128, KD * TLOC], BF16)
        qT = asb("qT", [128, KD * TLOC], BF16)
        gbf = asb("gbf", [128, NCH * D], BF16)
        WA = [asb(f"WA{i}", [128, KD * D], BF16) for i in range(2)]
        bg_sb = asb("bg_sb", [1, D], BF16)
        kv_rhs = asb("kv_rhs", [128, KD * 128], BF16)
        ksbd = asb("ksbd", [128, 16], BF16)
        obuf = asb("obuf", [128, D], F32)
        dzs = asb("dzs", [128, 16], F32)
        kvks = asb("kvks", [65, H * DH], F32)
        stats = asb("stats", [128, 4 * NCH], F32)
        stats2 = asb("stats2", [128, 4 * NCH], F32)
        lg_sb = asb("lg_sb", [128, NCH * E], F32)
        xcp = [asb(f"xc{i}", [128, D], F32) for i in range(3)]
        scratch = asb("scratch", [128, D], F32)
        sq = scratch
        elw = scratch[0:128, 0:512]
        xnc = [asb(f"xnc{i}", [128, D], BF16) for i in range(2)]
        kbf = [asb(f"kbf{i}", [128, D], BF16) for i in range(2)]
        vbf = [asb(f"vbf{i}", [128, D], BF16) for i in range(2)]
        zsb = asb("zsb", [128, H], F32)
        attw = asb("attw", [128, D], BF16)
        att = [asb(f"att{i}", [128, D], BF16) for i in range(2)]
        attT = asb("attT", [128, KD * 128], BF16)
        xl2b = asb("xl2b", [128, D], BF16)
        xl2T = asb("xl2T", [128, KD * 128], F32)

        for apd, dst in ((identFd, idF), (sel16d, sel16), (eidd, eid)):
            S.op("sync", lambda q, a=apd, d=dst: q.dma_start(out=d[:], in_=a), inc=(LDX, 16))
        S.op("sync", lambda q: q.dma_start(
            out=wgr_sb[:].rearrange("p (k e) -> p k e", k=KD),
            in_=wgrd.rearrange("(k p) e -> p k e", p=128)), inc=(LDX, 16))
        consts_sync = S.val(LDX)

        for apd, dst in ((identBd, idB), (triBd, triB), (bgd, bg_sb)):
            S.op("gpsimd", lambda g, a=apd, d=dst: g.dma_start(out=d[:], in_=a), inc=(LDW, 16))
        consts_gpw = S.val(LDW)
        S.op("gpsimd", lambda g: g.memset(onesB[:], 1.0))
        S.op("gpsimd", lambda g: g.memset(onesF[:], 1.0))
        S.op("gpsimd", lambda g: g.iota(
            iotaE[:].rearrange("p (f e) -> p f e", e=E), pattern=[[0, 128], [1, E]],
            base=0, channel_multiplier=0, allow_small_or_imprecise_dtypes=True))
        S.op("gpsimd", lambda g: g.iota(iotaF[:], pattern=[[1, 128]], base=0,
                                        channel_multiplier=0, allow_small_or_imprecise_dtypes=True))
        S.op("gpsimd", lambda g: g.iota(fp1[:], pattern=[[1, 128]], base=1,
                                        channel_multiplier=0, allow_small_or_imprecise_dtypes=True))
        S.op("gpsimd", lambda g: g.iota(iotaCH[:], pattern=[[128, 32]], base=0,
                                        channel_multiplier=0, allow_small_or_imprecise_dtypes=True))
        S.op("gpsimd", lambda g: g.iota(iotaCH32[:], pattern=[[1, 32]], base=0,
                                        channel_multiplier=0, allow_small_or_imprecise_dtypes=True))
        S.op("gpsimd", lambda g: g.iota(pcolF[:], pattern=[[0, 1]], base=0, channel_multiplier=1,
                                        allow_small_or_imprecise_dtypes=True), inc=(IG, 1))
        consts_gp = S.val(IG)
        S.op("gpsimd", lambda g: g.dma_start(
            out=WA[0][:].rearrange("p (k n) -> p k n", k=KD),
            in_=Wk.rearrange("(k p) n -> p k n", p=128)), inc=(WS0, 16))
        wk_ld = S.val(WS0)
        S.op("gpsimd", lambda g: g.dma_start(
            out=WA[1][:].rearrange("p (k n) -> p k n", k=KD),
            in_=Wv.rearrange("(k p) n -> p k n", p=128)), inc=(WS1, 16))
        wv_ld = S.val(WS1)

        ps1 = ExitStack()
        pstB = [ps1.enter_context(nc.psum_tensor(f"pstB{i}", [128, 1024], BF16)) for i in range(2)]

        xc_use = [0, 0, 0]
        ln_norm = {}
        p1_tt = {}
        p1_ev = {}
        for c in range(NCH):
            o = 4 * c
            i = c % 3
            w = [(VE, ln_norm[c - 3])] if c >= 3 else []
            xc_use[i] += 1
            S.op("sync", lambda q, c=c, i=i: q.dma_start(
                out=xcp[i][:], in_=x_my.rearrange("(c p) d -> p c d", p=128)[:, c, :]),
                waits=w, inc=(XC[i], 16))
            ldx_c = (XC[i], 16 * xc_use[i])
            sc_q = S.op("scalar", lambda s, i=i, o=o: s.activation(
                out=sq[:], in_=xcp[i][:], func=AF.Square, accum_out=stats[:, o + 1:o + 2]),
                waits=[ldx_c, (SC, S.val(SC))], inc=(SC, 1))
            sc_st = S.op("scalar", lambda s, i=i, o=o: s.activation(
                out=sq[:], in_=xcp[i][:], func=AF.Copy, accum_out=stats[:, o:o + 1]),
                waits=[(SC, sc_q)], inc=(SC, 1))
            S.op("vector", lambda v, o=o: v.tensor_scalar(
                stats[:, o:o + 1], stats[:, o:o + 1], 1.0 / D, None, ALU.mult),
                waits=[(SC, sc_st)])
            S.op("vector", lambda v, o=o: v.tensor_scalar(
                stats[:, o + 1:o + 2], stats[:, o + 1:o + 2], 1.0 / D, None, ALU.mult))
            S.op("vector", lambda v, o=o: v.tensor_tensor(
                stats[:, o + 3:o + 4], stats[:, o:o + 1], stats[:, o:o + 1], ALU.mult))
            S.op("vector", lambda v, o=o: v.tensor_sub(
                stats[:, o + 1:o + 2], stats[:, o + 1:o + 2], stats[:, o + 3:o + 4]))
            ve_var = S.op("vector", lambda v, o=o: v.tensor_scalar(
                stats[:, o + 1:o + 2], stats[:, o + 1:o + 2], 1e-5, None, ALU.add), inc=(VE, 1))
            sc_sq = S.op("scalar", lambda s, o=o: s.activation(
                out=stats[:, o + 2:o + 3], in_=stats[:, o + 1:o + 2], func=AF.Sqrt),
                waits=[(VE, ve_var)], inc=(SC, 1))
            ve_rc = S.op("vector", lambda v, o=o: v.reciprocal(
                stats[:, o + 2:o + 3], stats[:, o + 2:o + 3]),
                waits=[(SC, sc_sq)], inc=(VE, 1))
            w = [(TT, p1_tt[(c - 2, KD - 1)])] if c >= 2 else []
            w.append((VE, ve_rc))
            ln_norm[c] = S.op("vector", lambda v, c=c, i=i, o=o: v.tensor_scalar(
                xnc[c % 2][:], xcp[i][:], stats[:, o:o + 1], stats[:, o + 2:o + 3],
                ALU.subtract, ALU.mult), waits=w, inc=(VE, 1))
            for j in range(KD):
                w = []
                if j == 0:
                    w.append((VE, ln_norm[c]))
                    if c == 0:
                        w.append((LDW, consts_gpw))
                idx = c * KD + j
                if idx >= 2:
                    pj = (c, j - 2) if j >= 2 else (c - 1, KD + j - 2)
                    w.append((VE, p1_ev[pj]))
                p1_tt[(c, j)] = S.op("tensor", lambda t, c=c, j=j: t.transpose(
                    pstB[j % 2][:, 0:128], xnc[c % 2][:, j * 128:(j + 1) * 128], idB[:]),
                    waits=w, inc=(TT, 1))
                p1_ev[(c, j)] = S.op("vector", lambda v, c=c, j=j: v.tensor_copy(
                    xnT[:, j * TLOC + c * 128: j * TLOC + (c + 1) * 128], pstB[j % 2][:, 0:128]),
                    waits=[(TT, p1_tt[(c, j)])], inc=(VE, 1))
        p1_done = S.val(VE)

        ps1.close()
        ps2 = ExitStack()
        kvps = ps2.enter_context(nc.psum_tensor("kvps", [64, 1024], F32))
        ksA = ps2.enter_context(nc.psum_tensor("ksA", [128, 512], F32))
        ksB = ps2.enter_context(nc.psum_tensor("ksB", [128, 512], F32))
        prj = [ps2.enter_context(nc.psum_tensor(f"prj{i}", [128, 512], F32)) for i in range(2)]

        def proj_mms(wsb, c, n, pbuf, w0, bias=False):
            for j in range(KD):
                st = (j == 0)
                sp = (j == KD - 1) and not bias
                S.op("tensor", lambda t, c=c, n=n, j=j, wsb=wsb, pbuf=pbuf, st=st, sp=sp: t.matmul(
                    pbuf[:], xnT[:, j * TLOC + c * 128: j * TLOC + (c + 1) * 128],
                    wsb[:, j * D + n * 512: j * D + (n + 1) * 512], start=st, stop=sp),
                    waits=w0 if j == 0 else [])
            if bias:
                S.op("tensor", lambda t, n=n, pbuf=pbuf: t.matmul(
                    pbuf[:], onesB[0:1, :], bg_sb[:, n * 512:(n + 1) * 512],
                    start=False, stop=True))

        def elu_chain(dst, doff, pbuf, te_w):
            v1 = S.op("vector", lambda v, pbuf=pbuf: v.tensor_scalar_min(elw, pbuf[:], 0.0),
                      waits=[(TE, te_w), (SC, S.val(SC))], inc=(VE, 1))
            s1 = S.op("scalar", lambda s: s.activation(out=elw, in_=elw, func=AF.Exp),
                      waits=[(VE, v1)], inc=(SC, 1))
            return S.op("vector", lambda v, dst=dst, doff=doff, pbuf=pbuf: v.scalar_tensor_tensor(
                dst[:, doff:doff + 512], pbuf[:], 0.0, elw, ALU.max, ALU.add),
                waits=[(SC, s1)], inc=(VE, 1))

        kvacc = {}
        k_done = {}
        v_done = {}
        for c in range(NCH):
            for n in range(2):
                w = [(VE, p1_done), (WS0, wk_ld)]
                if c > 0:
                    w.append((VE, v_done[(c - 1, n)]))
                if c >= 2:
                    w.append((TE, kvacc[c - 2]))
                if debug and c == 2:
                    w.append((DBG, kbf_dump_done))
                proj_mms(WA[0], c, n, prj[n], w)
                te_k = S.mark("tensor", TE)
                k_done[(c, n)] = elu_chain(kbf[c % 2], n * 512, prj[n], te_k)
            for n in range(2):
                w = [(VE, k_done[(c, n)]), (WS1, wv_ld)]
                if c >= 2:
                    w.append((TE, kvacc[c - 2]))
                proj_mms(WA[1], c, n, prj[n], w)
                te_v = S.mark("tensor", TE)
                v_done[(c, n)] = S.op("vector", lambda v, c=c, n=n: v.tensor_copy(
                    vbf[c % 2][:, n * 512:(n + 1) * 512], prj[n][:]),
                    waits=[(TE, te_v)], inc=(VE, 1))
            w = [(VE, v_done[(c, 1)])]
            for h in range(H):
                S.op("tensor", lambda t, c=c, h=h: t.matmul(
                    kvps[:, h * 64:(h + 1) * 64], kbf[c % 2][:, h * 64:(h + 1) * 64],
                    vbf[c % 2][:, h * 64:(h + 1) * 64],
                    start=(c == 0 and h % 8 == 0), stop=(c == NCH - 1 and h % 8 == 7)),
                    waits=w if h == 0 else [])
            for n, tgt in ((0, ksA), (1, ksB)):
                S.op("tensor", lambda t, c=c, n=n, tgt=tgt: t.matmul(
                    tgt[0:1, :], onesB[:, 0:1], kbf[c % 2][:, n * 512:(n + 1) * 512],
                    start=(c == 0), stop=(c == NCH - 1)))
            kvacc[c] = S.mark("tensor", TE)
            if debug and c == 0:
                S.op("sync", lambda q: q.dma_start(out=dbg["kbf0"], in_=kbf[0][:]),
                     waits=[(VE, v_done[(0, 1)])], inc=(DBG, 16))
                kbf_dump_done = S.op("sync", lambda q: q.dma_start(
                    out=dbg["vbf0"], in_=vbf[0][:]), inc=(DBG, 16))
        kv_te = kvacc[NCH - 1]

        S.op("vector", lambda v: v.tensor_copy(kvks[0:64, :], kvps[:]), waits=[(TE, kv_te)])
        S.op("vector", lambda v: v.tensor_copy(kvks[64:65, 0:512], ksA[0:1, :]))
        ve_kv = S.op("vector", lambda v: v.tensor_copy(kvks[64:65, 512:1024], ksB[0:1, :]),
                     inc=(VE, 1))
        S.op("sync", lambda q: q.dma_start(out=kv_din, in_=kvks[:]),
             waits=[(VE, ve_kv)], inc=(STKV, 16))
        ag_kv = S.op("gpsimd", lambda g: g.collective_compute(
            "AllReduce", ALU.add, replica_groups=[[0, 1], [2, 3], [4, 5], [6, 7]],
            ins=[kv_din], outs=[kv_dout]), waits=[(STKV, 16)], inc=(AG, 1))
        mz_kv = S.op("gpsimd", lambda g: g.memset(kv_rhs[:], 0.0))
        mz_kv = S.op("gpsimd", lambda g: g.memset(ksbd[:], 0.0), inc=(IG, 1))
        S.op("gpsimd", lambda g: g.dma_start(
            out=kv_rhs[0:64, :].rearrange("p (cb j) -> p cb j", j=128)[:, :, 0:64],
            in_=kv_dout[0:64, :].rearrange("p (cb two j) -> p cb two j", two=2, j=64)
            [:, :, 0:1, :].rearrange("p cb o j -> p cb (o j)")),
            waits=[(AG, ag_kv), (IG, mz_kv)], inc=(KVL, 16))
        S.op("gpsimd", lambda g: g.dma_start(
            out=kv_rhs[64:128, :].rearrange("p (cb j) -> p cb j", j=128)[:, :, 64:128],
            in_=kv_dout[0:64, :].rearrange("p (cb two j) -> p cb two j", two=2, j=64)
            [:, :, 1:2, :].rearrange("p cb o j -> p cb (o j)")), inc=(KVL, 16))
        S.op("gpsimd", lambda g: g.dma_start(
            out=ksbd[0:64, :].rearrange("p (cb o) -> p cb o", o=2)[:, :, 0:1],
            in_=kv_dout[64:65, :].rearrange("o (cb two d) -> d cb (two o)", two=2, d=64)[:, :, 0:1],
            allow_slow_non_contiguous=True), inc=(KVL, 16))
        S.op("gpsimd", lambda g: g.dma_start(
            out=ksbd[64:128, :].rearrange("p (cb o) -> p cb o", o=2)[:, :, 1:2],
            in_=kv_dout[64:65, :].rearrange("o (cb two d) -> d cb (two o)", two=2, d=64)[:, :, 1:2],
            allow_slow_non_contiguous=True), inc=(KVL, 16))
        kvrhs_ld = S.val(KVL)

        S.op("gpsimd", lambda g: g.dma_start(
            out=WA[0][:].rearrange("p (k n) -> p k n", k=KD),
            in_=Wq.rearrange("(k p) n -> p k n", p=128)), waits=[(TE, kv_te)], inc=(WS0, 16))
        wq_ld = S.val(WS0)
        S.op("gpsimd", lambda g: g.dma_start(
            out=WA[1][:].rearrange("p (k n) -> p k n", k=KD),
            in_=Wg.rearrange("(k p) n -> p k n", p=128)), inc=(WS1, 16))
        wg_ld = S.val(WS1)

        q_done = {}
        te_q_last = 0
        for m in range(KD):
            for nt in range(4):
                i = m * 4 + nt
                pb = prj[i % 2]
                w = [(WS0, wq_ld), (VE, v_done[(NCH - 1, i % 2)])]
                if i >= 2:
                    w.append((VE, q_done[i - 2]))
                for j in range(KD):
                    S.op("tensor", lambda t, m=m, nt=nt, j=j, pb=pb: t.matmul(
                        pb[:], WA[0][:, j * D + m * 128: j * D + (m + 1) * 128],
                        xnT[:, j * TLOC + nt * 512: j * TLOC + (nt + 1) * 512],
                        start=(j == 0), stop=(j == KD - 1)), waits=w if j == 0 else [])
                te_q_last = S.mark("tensor", TE)
                q_done[i] = elu_chain(qT, m * TLOC + nt * 512, pb, te_q_last)
        g_done = {}
        for c in range(NCH):
            for n in range(2):
                w = [(WS1, wg_ld), (VE, q_done[KD * 4 - 2 + n])]
                if c > 0:
                    w.append((SC, g_done[(c - 1, n)]))
                proj_mms(WA[1], c, n, prj[n], w, bias=True)
                te_g = S.mark("tensor", TE)
                g_done[(c, n)] = S.op("scalar", lambda s, c=c, n=n: s.activation(
                    out=gbf[:, c * D + n * 512:c * D + (n + 1) * 512], in_=prj[n][:],
                    func=AF.Sigmoid), waits=[(TE, te_g)], inc=(SC, 1))
        S.op("gpsimd", lambda g: g.dma_start(
            out=WA[0][:].rearrange("p (k n) -> p k n", k=KD),
            in_=Wo.rearrange("(k p) n -> p k n", p=128)), waits=[(TE, te_q_last)], inc=(WS0, 16))
        wo_ld = S.val(WS0)

        if debug:
            S.op("sync", lambda q: q.dma_start(out=dbg["xnT"], in_=xnT[:]),
                 waits=[(VE, p1_done)], inc=(DBG, 16))
            S.op("sync", lambda q: q.dma_start(out=dbg["qT"], in_=qT[:]),
                 waits=[(VE, q_done[KD * 4 - 1])], inc=(DBG, 16))
            S.op("sync", lambda q: q.dma_start(out=dbg["gbf"], in_=gbf[:]),
                 waits=[(SC, g_done[(NCH - 1, 1)])], inc=(DBG, 16))
            S.op("sync", lambda q: q.dma_start(out=dbg["kvr"], in_=kv_rhs[:]),
                 waits=[(KVL, kvrhs_ld)], inc=(DBG, 16))
        ps2.close()
        ps4 = ExitStack()
        outps = ps4.enter_context(nc.psum_tensor("outps", [128, 1024], F32))
        denps = ps4.enter_context(nc.psum_tensor("denps", [128, 512], F32))
        yps = [ps4.enter_context(nc.psum_tensor(f"yps{i}", [128, 512], F32)) for i in range(2)]
        pstB2 = [ps4.enter_context(nc.psum_tensor(f"pstB2{i}", [128, 1024], BF16)) for i in range(2)]

        att_ve_prev = 0
        attT_used_prev = 0
        x1_used_prev = {}
        tt4 = {}
        ev4 = {}
        stb_use = 0
        for c in range(NCH):
            w = [(KVL, kvrhs_ld)]
            if c > 0:
                w.append((VE, att_ve_prev))
            for m in range(KD):
                S.op("tensor", lambda t, m=m, c=c: t.matmul(
                    outps[:, m * 128:(m + 1) * 128],
                    qT[:, m * TLOC + c * 128: m * TLOC + (c + 1) * 128],
                    kv_rhs[:, m * 128:(m + 1) * 128],
                    start=True, stop=True),
                    waits=w if m == 0 else [])
                S.op("tensor", lambda t, m=m, c=c: t.matmul(
                    denps[:, 2 * m:2 * m + 2],
                    qT[:, m * TLOC + c * 128: m * TLOC + (c + 1) * 128],
                    ksbd[:, 2 * m:2 * m + 2],
                    start=True, stop=True))
            te_out = S.mark("tensor", TE)
            wg_ep = [(TE, te_out)]
            if c == 0:
                wg_ep.append((SC, g_done[(NCH - 1, 1)]))
            S.op("vector", lambda v: v.tensor_copy(obuf[:], outps[:]), waits=wg_ep, inc=(VE, 1))
            ve_dzs = S.op("vector", lambda v: v.tensor_copy(dzs[:], denps[:, 0:16]), inc=(VE, 1))
            ve_zm = S.op("vector", lambda v: v.tensor_scalar_max(zsb[:], dzs[:], 1e-6),
                         waits=[(VE, ve_dzs)], inc=(VE, 1))
            ve_zr = S.op("vector", lambda v: v.reciprocal(zsb[:], zsb[:]),
                         waits=[(VE, ve_zm)], inc=(VE, 1))
            S.op("vector", lambda v: v.tensor_scalar(zsb[:], zsb[:], SCALE, None, ALU.mult),
                 waits=[(VE, ve_zr)])
            S.op("vector", lambda v: v.tensor_tensor(
                attw[:].rearrange("p (h e) -> p h e", e=64),
                obuf[:].rearrange("p (h e) -> p h e", e=64),
                zsb[:].rearrange("p (h o) -> p h o", o=1).broadcast_to((128, H, 64)),
                ALU.mult))
            att_ve_prev = S.op("vector", lambda v, c=c: v.tensor_tensor(
                att[c % 2][:], attw[:], gbf[:, c * D:(c + 1) * D], ALU.mult), inc=(VE, 1))
            if os.environ.get("KSKIP") == "tt":
                continue
            if debug and c == 0:
                S.op("sync", lambda q: q.dma_start(out=dbg["att0"], in_=att[0][:]),
                     waits=[(VE, att_ve_prev)], inc=(DBG, 16))
                S.op("sync", lambda q: q.dma_start(out=dbg["obuf0"], in_=obuf[:]), inc=(DBG, 16))
                S.op("sync", lambda q: q.dma_start(out=dbg["dzs0"], in_=dzs[:]), inc=(DBG, 16))
                S.op("sync", lambda q: q.dma_start(out=dbg["zsb0"], in_=zsb[:]), inc=(DBG, 16))
                S.op("sync", lambda q: q.dma_start(out=dbg["ksbd"], in_=ksbd[:]), inc=(DBG, 16))
            for j in range(KD):
                w = []
                if j == 0:
                    w.append((VE, att_ve_prev))
                    if c > 0:
                        w.append((TE, attT_used_prev))
                idx = c * KD + j
                if idx >= 2:
                    pj = (c, j - 2) if j >= 2 else (c - 1, KD + j - 2)
                    w.append((VE, ev4[pj]))
                tt4[(c, j)] = S.op("tensor", lambda t, c=c, j=j: t.transpose(
                    pstB2[j % 2][:, 0:128], att[c % 2][:, j * 128:(j + 1) * 128], idB[:]),
                    waits=w, inc=(TT, 1))
                ev4[(c, j)] = S.op("vector", lambda v, j=j: v.tensor_copy(
                    attT[:, j * 128:(j + 1) * 128], pstB2[j % 2][:, 0:128]),
                    waits=[(TT, tt4[(c, j)])], inc=(VE, 1))
            i = c % 3
            xc_use[i] += 1
            S.op("sync", lambda q, c=c, i=i: q.dma_start(
                out=xcp[i][:], in_=x_my.rearrange("(c p) d -> p c d", p=128)[:, c, :]),
                waits=[(VE, ln_norm[NCH - 1])], inc=(XC[i], 16))
            x2_ld = (XC[i], 16 * xc_use[i])
            for n in range(2):
                w = [(VE, ev4[(c, KD - 1)]), (WS0, wo_ld)]
                if c > 0:
                    w.append((VE, x1_used_prev[n]))
                for j in range(KD):
                    S.op("tensor", lambda t, j=j, n=n: t.matmul(
                        yps[n][:], attT[:, j * 128:(j + 1) * 128],
                        WA[0][:, j * D + n * 512: j * D + (n + 1) * 512],
                        start=(j == 0), stop=(j == KD - 1)), waits=w if j == 0 else [])
            attT_used_prev = S.mark("tensor", TE)
            for n in range(2):
                x1_used_prev[n] = S.op("vector", lambda v, c=c, n=n, i=i: v.tensor_tensor(
                    x1c[c % 2][:, n * 512:(n + 1) * 512], xcp[i][:, n * 512:(n + 1) * 512],
                    yps[n][:], ALU.add), waits=[(TE, attT_used_prev), x2_ld], inc=(VE, 1))
            x1_ve = x1_used_prev[1]
            o = 4 * c
            sc_q = S.op("scalar", lambda s, c=c, o=o: s.activation(
                out=sq[:], in_=x1c[c % 2][:], func=AF.Square, accum_out=stats2[:, o + 1:o + 2]),
                waits=[(VE, x1_ve), (SC, S.val(SC))], inc=(SC, 1))
            sc_st = S.op("scalar", lambda s, c=c, o=o: s.activation(
                out=sq[:], in_=x1c[c % 2][:], func=AF.Copy, accum_out=stats2[:, o:o + 1]),
                waits=[(SC, sc_q)], inc=(SC, 1))
            S.op("vector", lambda v, o=o: v.tensor_scalar(
                stats2[:, o:o + 1], stats2[:, o:o + 1], 1.0 / D, None, ALU.mult),
                waits=[(SC, sc_st)])
            S.op("vector", lambda v, o=o: v.tensor_scalar(
                stats2[:, o + 1:o + 2], stats2[:, o + 1:o + 2], 1.0 / D, None, ALU.mult))
            S.op("vector", lambda v, o=o: v.tensor_tensor(
                stats2[:, o + 3:o + 4], stats2[:, o:o + 1], stats2[:, o:o + 1], ALU.mult))
            S.op("vector", lambda v, o=o: v.tensor_sub(
                stats2[:, o + 1:o + 2], stats2[:, o + 1:o + 2], stats2[:, o + 3:o + 4]))
            ve_var = S.op("vector", lambda v, o=o: v.tensor_scalar(
                stats2[:, o + 1:o + 2], stats2[:, o + 1:o + 2], 1e-5, None, ALU.add), inc=(VE, 1))
            sc_sq = S.op("scalar", lambda s, o=o: s.activation(
                out=stats2[:, o + 2:o + 3], in_=stats2[:, o + 1:o + 2], func=AF.Sqrt),
                waits=[(VE, ve_var)], inc=(SC, 1))
            ve_rc2 = S.op("vector", lambda v, o=o: v.reciprocal(
                stats2[:, o + 2:o + 3], stats2[:, o + 2:o + 3]),
                waits=[(SC, sc_sq)], inc=(VE, 1))
            w = [(STB, 16 * stb_use)] if stb_use else []
            w.append((VE, x1_ve))
            w.append((VE, ve_rc2))
            xl2b_ve = S.op("vector", lambda v, c=c, o=o: v.tensor_scalar(
                xl2b[:], x1c[c % 2][:], stats2[:, o:o + 1], stats2[:, o + 2:o + 3],
                ALU.subtract, ALU.mult), waits=w, inc=(VE, 1))
            stb_use += 1
            S.op("sync", lambda q, c=c: q.dma_start(
                out=agt_in.rearrange("(c p) d -> p c d", p=128)[:, c, :], in_=xl2b[:]),
                waits=[(VE, xl2b_ve)], inc=(STB, 16))
            S.op("sync", lambda q, c=c: q.dma_start(
                out=x1d.rearrange("(c p) d -> p c d", p=128)[:, c, :], in_=x1c[c % 2][:]),
                inc=(STX1, 16))
            if debug:
                S.op("sync", lambda q, c=c: q.dma_start(
                    out=dbg["x1"].rearrange("(c p) d -> p c d", p=128)[:, c, :],
                    in_=x1c[c % 2][:]), inc=(DBG, 16))
        p4_done_ve = S.val(VE)
        p4_done_sc = S.val(SC)
        x1_all_st = S.val(STX1)
        agt_all_st = S.val(STB)

        ps4.close()
        ps5 = ExitStack()
        pstF = [ps5.enter_context(nc.psum_tensor(f"pstF{i}", [128, 512], F32)) for i in range(2)]
        lgps = ps5.enter_context(nc.psum_tensor("lgps", [128, 512], F32))

        x1_use = [0, 0]
        lg_ev_prev = 0
        xl2f_ve = {}
        tt5 = {}
        ev5 = {}
        te_lg = 0
        for c in range(NCH):
            o = 4 * c
            i = c % 2
            w = [(STX1, x1_all_st), (VE, p4_done_ve), (SC, p4_done_sc)]
            if c >= 2:
                w.append((VE, xl2f_ve[c - 2]))
            x1_use[i] += 1
            S.op("sync", lambda q, c=c, i=i: q.dma_start(
                out=x1c[i][:], in_=x1d.rearrange("(c p) d -> p c d", p=128)[:, c, :]),
                waits=w, inc=(X1S[i], 16))
            x5_ld = (X1S[i], 16 * x1_use[i])
            w = [x5_ld]
            if c >= 2:
                w.append((TT, tt5[(c - 2, KD - 1)]))
            xl2f_ve[c] = S.op("vector", lambda v, c=c, i=i, o=o: v.tensor_scalar(
                xl2f[i][:], x1c[i][:], stats2[:, o:o + 1], stats2[:, o + 2:o + 3],
                ALU.subtract, ALU.mult), waits=w, inc=(VE, 1))
            for j in range(KD):
                w = []
                if j == 0:
                    w.append((VE, xl2f_ve[c]))
                    if c > 0:
                        w.append((TE, te_lg))
                idx = c * KD + j
                if idx >= 2:
                    pj = (c, j - 2) if j >= 2 else (c - 1, KD + j - 2)
                    w.append((VE, ev5[pj]))
                tt5[(c, j)] = S.op("tensor", lambda t, c=c, i=i, j=j: t.transpose(
                    pstF[j % 2][:, 0:128], xl2f[i][:, j * 128:(j + 1) * 128], idF[:]),
                    waits=w, inc=(TT, 1))
                ev5[(c, j)] = S.op("vector", lambda v, j=j: v.tensor_copy(
                    xl2T[:, j * 128:(j + 1) * 128], pstF[j % 2][:, 0:128]),
                    waits=[(TT, tt5[(c, j)])], inc=(VE, 1))
            w = [(VE, ev5[(c, KD - 1)]), (LDX, consts_sync)]
            if c > 0:
                w.append((VE, lg_ev_prev))
            for j in range(KD):
                S.op("tensor", lambda t, j=j: t.matmul(
                    lgps[:, 0:E], xl2T[:, j * 128:(j + 1) * 128],
                    wgr_sb[:, j * E:(j + 1) * E], start=(j == 0), stop=(j == KD - 1)),
                    waits=w if j == 0 else [])
            te_lg = S.mark("tensor", TE)
            lg_ev_prev = S.op("vector", lambda v, c=c: v.tensor_copy(
                lg_sb[:, c * E:(c + 1) * E], lgps[:, 0:E]), waits=[(TE, te_lg)], inc=(VE, 1))
        S.op("sync", lambda q: q.dma_start(
            out=agl_in.rearrange("(c p) e -> p c e", p=128),
            in_=lg_sb[:].rearrange("p (c e) -> p c e", e=E)),
            waits=[(VE, lg_ev_prev)], inc=(STLG, 16))

        ag_tok = S.op("gpsimd", lambda g: g.collective_compute(
            "AllGather", ALU.bypass, replica_groups=[list(range(NCORES))],
            ins=[agt_in], outs=[agt_out]), waits=[(STB, agt_all_st)], inc=(AG, 1))
        ag_lg = S.op("gpsimd", lambda g: g.collective_compute(
            "AllGather", ALU.bypass, replica_groups=[list(range(NCORES))],
            ins=[agl_in], outs=[agl_out]), waits=[(STLG, 16)], inc=(AG, 1))
        ag_both = S.val(AG)

        ps5.close()
        att_scope.close()

        ffw_scope = ExitStack()
        weg_sb = ffw_scope.enter_context(nc.sbuf_tensor("weg_sb", [128, KD * FF], BF16))
        S.op("gpsimd", lambda g: g.dma_start(
            out=weg_sb[:].rearrange("p (k n) -> p k n", k=KD),
            in_=wegd.rearrange("(k p) n -> p k n", p=128)),
            waits=[(VE, lg_ev_prev), (STB, agt_all_st), (STLG, 16)], inc=(WEG, 16))

        rt_scope = ExitStack()
        rsb = lambda name, shape, dt: rt_scope.enter_context(nc.sbuf_tensor(name, shape, dt))
        lgt = rsb("lgt", [128, 128 * E], F32)
        OH1 = rsb("OH1", [128, 128 * E], F32)
        OH2 = rsb("OH2", [128, 128 * E], F32)
        SCN1 = rsb("SCN1", [128, E * 128], F32)
        SCN2 = rsb("SCN2", [128, E * 128], F32)
        LOC1 = rsb("LOC1", [128, E * 128], F32)
        LOC2 = rsb("LOC2", [128, E * 128], F32)
        TMPA = rsb("TMPA", [128, 128 * E], F32)
        TMPB = rsb("TMPB", [128, 128 * E], F32)
        P = {nm: rsb(nm, [128, 128], F32) for nm in
             ("mx", "idx1", "idx2", "l2v", "e2v", "g1n", "g2n", "loc1s", "loc2s",
              "K1", "K2", "F1", "F2", "G1K", "G2K", "SEL", "LM", "HH", "LL")}
        rt = rsb("rt", [128, 16], F32)
        rtb = rsb("rtb", [128, 16], BF16)
        offsb = rsb("offsb", [128, 16], F32)
        t1sb = rsb("t1sb", [1, 8], F32)
        A32 = rsb("A32", [128, 128 * 32], BF16)
        HIB = rsb("HIB", [128, 128 * 32], BF16)
        GB = rsb("GB", [128, 128 * 64], BF16)
        LOB = rsb("LOB", [128, 128 * 128], BF16)
        tabsb = rsb("tabsb", [128, 64], F32)
        metasb = rsb("metasb", [16, 512], F32)

        ps6 = ExitStack()
        offs_ps = ps6.enter_context(nc.psum_tensor("offs_ps", [128, 512], F32))
        t1ps = ps6.enter_context(nc.psum_tensor("t1ps", [128, 512], F32))
        tabps = ps6.enter_context(nc.psum_tensor("tabps", [128, 512], F32))
        metaps = ps6.enter_context(nc.psum_tensor("metaps", [128, 512], F32))

        S.op("sync", lambda q: q.dma_start(
            out=lgt[:], in_=agl_out.rearrange("(p f) e -> p (f e)", p=128)),
            waits=[(AG, ag_both)], inc=(RLG, 16))

        lgv = lambda b: b[:].rearrange("p (f e) -> p f e", e=E)
        pv = lambda b: b[:].rearrange("p (f o) -> p f o", o=1)

        S.op("vector", lambda v: v.reduce_max(P["mx"][:], lgv(lgt), axis=AX.X),
             waits=[(VE, S.val(VE)), (RLG, 16), (IG, consts_gp)], inc=(VE, 1))
        S.op("vector", lambda v: v.tensor_tensor(
            lgv(TMPA), lgv(lgt), pv(P["mx"]).broadcast_to((128, 128, E)), ALU.is_equal), inc=(VE, 1), waits=[(VE, S.val(VE))])
        S.op("vector", lambda v: v.scalar_tensor_tensor(
            TMPB[:], TMPA[:], -1000.0, iotaE[:], ALU.mult, ALU.add), inc=(VE, 1), waits=[(VE, S.val(VE))])
        S.op("vector", lambda v: v.tensor_reduce(P["idx1"][:], lgv(TMPB), axis=AX.X, op=ALU.min), inc=(VE, 1), waits=[(VE, S.val(VE))])
        S.op("vector", lambda v: v.tensor_scalar(P["idx1"][:], P["idx1"][:], 1000.0, None, ALU.add), inc=(VE, 1), waits=[(VE, S.val(VE))])
        S.op("vector", lambda v: v.tensor_tensor(
            lgv(OH1), lgv(iotaE), pv(P["idx1"]).broadcast_to((128, 128, E)), ALU.is_equal), inc=(VE, 1), waits=[(VE, S.val(VE))])
        S.op("vector", lambda v: v.scalar_tensor_tensor(
            TMPA[:], OH1[:], -1e30, lgt[:], ALU.mult, ALU.add), inc=(VE, 1), waits=[(VE, S.val(VE))])
        S.op("vector", lambda v: v.reduce_max(P["l2v"][:], lgv(TMPA), axis=AX.X), inc=(VE, 1), waits=[(VE, S.val(VE))])
        S.op("vector", lambda v: v.tensor_tensor(
            lgv(TMPB), lgv(TMPA), pv(P["l2v"]).broadcast_to((128, 128, E)), ALU.is_equal), inc=(VE, 1), waits=[(VE, S.val(VE))])
        S.op("vector", lambda v: v.scalar_tensor_tensor(
            TMPA[:], TMPB[:], -1000.0, iotaE[:], ALU.mult, ALU.add), inc=(VE, 1), waits=[(VE, S.val(VE))])
        S.op("vector", lambda v: v.tensor_reduce(P["idx2"][:], lgv(TMPA), axis=AX.X, op=ALU.min), inc=(VE, 1), waits=[(VE, S.val(VE))])
        S.op("vector", lambda v: v.tensor_scalar(P["idx2"][:], P["idx2"][:], 1000.0, None, ALU.add), inc=(VE, 1), waits=[(VE, S.val(VE))])
        S.op("vector", lambda v: v.tensor_tensor(
            lgv(OH2), lgv(iotaE), pv(P["idx2"]).broadcast_to((128, 128, E)), ALU.is_equal), inc=(VE, 1), waits=[(VE, S.val(VE))])
        ve_d21 = S.op("vector", lambda v: v.tensor_sub(P["e2v"][:], P["l2v"][:], P["mx"][:]),
                      inc=(VE, 1), waits=[(VE, S.val(VE))])
        sc_e2 = S.op("scalar", lambda s: s.activation(
            out=P["e2v"][:], in_=P["e2v"][:], func=AF.Exp),
            waits=[(VE, ve_d21), (SC, S.val(SC))], inc=(SC, 1))
        ve_g1a = S.op("vector", lambda v: v.tensor_scalar(
            P["g1n"][:], P["e2v"][:], 1.0, None, ALU.add),
            waits=[(VE, S.val(VE)), (SC, sc_e2)], inc=(VE, 1))
        ve_g1r = S.op("vector", lambda v: v.reciprocal(P["g1n"][:], P["g1n"][:]),
                      waits=[(VE, S.val(VE)), (VE, ve_g1a)], inc=(VE, 1))
        S.op("vector", lambda v: v.tensor_tensor(
            P["g2n"][:], P["e2v"][:], P["g1n"][:], ALU.mult), waits=[(VE, S.val(VE)), (VE, ve_g1r)], inc=(VE, 1))

        def plane(buf, e):
            return buf[:].rearrange("p (f e) -> p e f", e=E)[:, e:e + 1, :].rearrange(
                "p o f -> p (o f)")
        for e in range(E):
            S.op("vector", lambda v, e=e, pl=plane: v.tensor_tensor_scan(
                SCN1[:, e * 128:(e + 1) * 128], onesF[:], pl(OH1, e), 0.0, ALU.mult, ALU.add), inc=(VE, 1), waits=[(VE, S.val(VE))])
            S.op("vector", lambda v, e=e, pl=plane: v.tensor_tensor_scan(
                SCN2[:, e * 128:(e + 1) * 128], onesF[:], pl(OH2, e), 0.0, ALU.mult, ALU.add), inc=(VE, 1), waits=[(VE, S.val(VE))])
        S.op("vector", lambda v: v.tensor_copy(
            rt[:, 0:8].rearrange("p (e o) -> p e o", o=1),
            SCN1[:].rearrange("p (e f) -> p e f", f=128)[:, :, 127:128]), inc=(VE, 1), waits=[(VE, S.val(VE))])
        S.op("vector", lambda v: v.tensor_copy(
            rt[:, 8:16].rearrange("p (e o) -> p e o", o=1),
            SCN2[:].rearrange("p (e f) -> p e f", f=128)[:, :, 127:128]), inc=(VE, 1), waits=[(VE, S.val(VE))])
        ve_rtb = S.op("vector", lambda v: v.tensor_copy(rtb[:], rt[:]), inc=(VE, 1), waits=[(VE, S.val(VE))])
        S.op("tensor", lambda t: t.matmul(offs_ps[:, 0:16], triB[:], rtb[:], start=True, stop=False),
             waits=[(VE, ve_rtb)])
        te_t1 = S.op("tensor", lambda t: t.matmul(
            t1ps[0:1, 0:8], onesB[:, 0:1], rtb[:, 0:8], start=True, stop=True), inc=(TE, 1))
        ve_t1 = S.op("vector", lambda v: v.tensor_copy(t1sb[:], t1ps[0:1, 0:8]),
                     waits=[(VE, S.val(VE)), (TE, te_t1)], inc=(VE, 1))
        te_offs = S.op("tensor", lambda t: t.matmul(
            offs_ps[:, 8:16], onesF[0:1, :], t1sb[:], start=False, stop=True),
            waits=[(VE, ve_t1)], inc=(TE, 1))
        ve_offs = S.op("vector", lambda v: v.tensor_copy(offsb[:], offs_ps[:, 0:16]),
                       waits=[(VE, S.val(VE)), (TE, te_offs)], inc=(VE, 1))
        for e in range(E):
            S.op("vector", lambda v, e=e: v.tensor_scalar(
                LOC1[:, e * 128:(e + 1) * 128], SCN1[:, e * 128:(e + 1) * 128],
                offsb[:, e:e + 1], None, ALU.add), waits=[(VE, S.val(VE)), (VE, ve_offs)] if e == 0 else [], inc=(VE, 1))
            S.op("vector", lambda v, e=e: v.tensor_scalar(
                LOC2[:, e * 128:(e + 1) * 128], SCN2[:, e * 128:(e + 1) * 128],
                offsb[:, 8 + e:9 + e], None, ALU.add), inc=(VE, 1), waits=[(VE, S.val(VE))])
        pm = lambda b: b[:].rearrange("p (e f) -> p e f", f=128)
        ohm = lambda b: b[:].rearrange("p (f e) -> p e f", e=E)
        S.op("vector", lambda v: v.tensor_sub(pm(LOC1), pm(LOC1), ohm(OH1)), inc=(VE, 1), waits=[(VE, S.val(VE))])
        S.op("vector", lambda v: v.tensor_sub(pm(LOC2), pm(LOC2), ohm(OH2)), inc=(VE, 1), waits=[(VE, S.val(VE))])
        S.op("vector", lambda v: v.tensor_tensor(pm(TMPA), pm(LOC1), ohm(OH1), ALU.mult), inc=(VE, 1), waits=[(VE, S.val(VE))])
        S.op("vector", lambda v: v.reduce_sum(
            P["loc1s"][:], TMPA[:].rearrange("p (e f) -> p f e", f=128), axis=AX.X), inc=(VE, 1), waits=[(VE, S.val(VE))])
        S.op("vector", lambda v: v.tensor_tensor(pm(TMPA), pm(LOC2), ohm(OH2), ALU.mult), inc=(VE, 1), waits=[(VE, S.val(VE))])
        S.op("vector", lambda v: v.reduce_sum(
            P["loc2s"][:], TMPA[:].rearrange("p (e f) -> p f e", f=128), axis=AX.X), inc=(VE, 1), waits=[(VE, S.val(VE))])
        S.op("vector", lambda v: v.tensor_single_scalar(P["K1"][:], P["loc1s"][:], float(C), ALU.is_lt), inc=(VE, 1), waits=[(VE, S.val(VE))])
        S.op("vector", lambda v: v.tensor_single_scalar(P["K2"][:], P["loc2s"][:], float(C), ALU.is_lt), inc=(VE, 1), waits=[(VE, S.val(VE))])
        S.op("vector", lambda v: v.tensor_scalar_min(P["LM"][:], P["loc1s"][:], float(C - 1)), inc=(VE, 1), waits=[(VE, S.val(VE))])
        S.op("vector", lambda v: v.scalar_tensor_tensor(
            P["F1"][:], P["idx1"][:], float(C), P["LM"][:], ALU.mult, ALU.add), inc=(VE, 1), waits=[(VE, S.val(VE))])
        S.op("vector", lambda v: v.tensor_scalar_min(P["LM"][:], P["loc2s"][:], float(C - 1)), inc=(VE, 1), waits=[(VE, S.val(VE))])
        S.op("vector", lambda v: v.scalar_tensor_tensor(
            P["F2"][:], P["idx2"][:], float(C), P["LM"][:], ALU.mult, ALU.add), inc=(VE, 1), waits=[(VE, S.val(VE))])
        S.op("vector", lambda v: v.tensor_tensor(P["G1K"][:], P["g1n"][:], P["K1"][:], ALU.mult), inc=(VE, 1), waits=[(VE, S.val(VE))])
        ve_rt1 = S.op("vector", lambda v: v.tensor_tensor(
            P["G2K"][:], P["g2n"][:], P["K2"][:], ALU.mult), inc=(VE, 1), waits=[(VE, S.val(VE))])

        if debug:
            for nm, src in (("lgt", lgt), ("loc1s", P["loc1s"]), ("loc2s", P["loc2s"]),
                            ("idx1", P["idx1"]), ("g1n", P["g1n"])):
                S.op("sync", lambda q, nm=nm, src=src: q.dma_start(out=dbg[nm], in_=src[:]),
                     waits=[(VE, ve_rt1)], inc=(DBG, 16))

        te_tab = {}
        for r, (idxP, locP, KP) in enumerate((("idx1", "loc1s", "K1"), ("idx2", "loc2s", "K2"))):
            w = [(VE, ve_rt1)]
            if r == 1:
                w.append((TE, te_tab[0]))
            S.op("vector", lambda v, idxP=idxP: v.tensor_scalar(
                P["SEL"][:], P[idxP][:], eid[:, 0:1], None, ALU.is_equal), waits=[(VE, S.val(VE))] + w, inc=(VE, 1))
            S.op("vector", lambda v, KP=KP: v.tensor_tensor(
                P["SEL"][:], P["SEL"][:], P[KP][:], ALU.mult), inc=(VE, 1), waits=[(VE, S.val(VE))])
            S.op("vector", lambda v, locP=locP: v.tensor_scalar(
                P["LM"][:], P[locP][:], 16000.0, None, ALU.subtract), inc=(VE, 1), waits=[(VE, S.val(VE))])
            S.op("vector", lambda v: v.tensor_tensor(P["LM"][:], P["LM"][:], P["SEL"][:], ALU.mult), inc=(VE, 1), waits=[(VE, S.val(VE))])
            S.op("vector", lambda v: v.tensor_scalar(P["LM"][:], P["LM"][:], 16000.0, None, ALU.add), inc=(VE, 1), waits=[(VE, S.val(VE))])
            S.op("vector", lambda v: v.tensor_tensor(
                A32[:].rearrange("p (f c) -> p f c", c=32),
                pv(P["LM"]).broadcast_to((128, 128, 32)),
                iotaCH[:].rearrange("p (o c) -> p o c", o=1).broadcast_to((128, 128, 32)),
                ALU.is_ge), inc=(VE, 1), waits=[(VE, S.val(VE))])
            S.op("vector", lambda v: v.reduce_sum(
                P["HH"][:], A32[:].rearrange("p (f c) -> p f c", c=32), axis=AX.X), inc=(VE, 1), waits=[(VE, S.val(VE))])
            S.op("vector", lambda v: v.tensor_scalar(P["HH"][:], P["HH"][:], 1.0, None, ALU.subtract), inc=(VE, 1), waits=[(VE, S.val(VE))])
            S.op("vector", lambda v: v.scalar_tensor_tensor(
                P["LL"][:], P["HH"][:], -128.0, P["LM"][:], ALU.mult, ALU.add), inc=(VE, 1), waits=[(VE, S.val(VE))])
            S.op("vector", lambda v: v.tensor_tensor(
                HIB[:].rearrange("p (f c) -> p f c", c=32),
                pv(P["HH"]).broadcast_to((128, 128, 32)),
                iotaCH32[:].rearrange("p (o c) -> p o c", o=1).broadcast_to((128, 128, 32)),
                ALU.is_equal), inc=(VE, 1), waits=[(VE, S.val(VE))])
            S.op("vector", lambda v: v.tensor_scalar(
                GB[:].rearrange("p (f c) -> p f c", c=64)[:, :, 0:32],
                HIB[:].rearrange("p (f c) -> p f c", c=32), pcolF[:, 0:1], None, ALU.mult), inc=(VE, 1), waits=[(VE, S.val(VE))])
            S.op("vector", lambda v: v.tensor_tensor(
                GB[:].rearrange("p (f c) -> p f c", c=64)[:, :, 32:64],
                HIB[:].rearrange("p (f c) -> p f c", c=32),
                fp1[:].rearrange("p (f o) -> p f o", o=1).broadcast_to((128, 128, 32)), ALU.mult), inc=(VE, 1), waits=[(VE, S.val(VE))])
            ve_lob = S.op("vector", lambda v: v.tensor_tensor(
                LOB[:].rearrange("p (f c) -> p f c", c=128),
                pv(P["LL"]).broadcast_to((128, 128, 128)),
                iotaF[:].rearrange("p (o c) -> p o c", o=1).broadcast_to((128, 128, 128)),
                ALU.is_equal), inc=(VE, 1), waits=[(VE, S.val(VE))])
            for f in range(128):
                S.op("tensor", lambda t, f=f, r=r: t.matmul(
                    tabps[:, 0:64], LOB[:, f * 128:(f + 1) * 128], GB[:, f * 64:(f + 1) * 64],
                    start=(r == 0 and f == 0), stop=(r == 1 and f == 127)),
                    waits=[(VE, ve_lob)] if f == 0 else [])
            te_tab[r] = S.mark("tensor", TE)
        ve_tab = S.op("vector", lambda v: v.tensor_copy(tabsb[:], tabps[:, 0:64]),
                      waits=[(VE, S.val(VE)), (TE, te_tab[1])], inc=(VE, 1))
        S.op("vector", lambda v: v.scalar_tensor_tensor(
            tabsb[:, 0:32], tabsb[:, 0:32], 128.0, tabsb[:, 32:64], ALU.mult, ALU.add),
            waits=[(VE, S.val(VE)), (VE, ve_tab)], inc=(VE, 1))
        S.op("vector", lambda v: v.tensor_scalar(
            tabsb[:, 0:32], tabsb[:, 0:32], 1.0, None, ALU.subtract), inc=(VE, 1), waits=[(VE, S.val(VE))])
        S.op("vector", lambda v: v.tensor_scalar_max(tabsb[:, 0:32], tabsb[:, 0:32], 0.0), inc=(VE, 1), waits=[(VE, S.val(VE))])
        ve_sidx = S.op("vector", lambda v: v.tensor_copy(SIDX[:], tabsb[:, 0:32]), inc=(VE, 1), waits=[(VE, S.val(VE))])
        if debug:
            S.op("sync", lambda q: q.dma_start(out=dbg["sidx"], in_=SIDX[:]),
                 waits=[(VE, ve_sidx)], inc=(DBG, 16))

        for i, plname in enumerate(("F1", "F2", "G1K", "G2K")):
            S.op("tensor", lambda t, i=i, plname=plname: t.matmul(
                metaps[0:16, 128 * i:128 * (i + 1)], sel16[:], P[plname][:],
                start=True, stop=True), waits=[(VE, ve_rt1)] if i == 0 else [])
        te_meta = S.mark("tensor", TE)
        ve_meta = S.op("vector", lambda v: v.tensor_copy(metasb[0:16, :], metaps[0:16, 0:512]),
                       waits=[(VE, S.val(VE)), (TE, te_meta)], inc=(VE, 1))
        for i in range(4):
            S.op("tensor", lambda t, i=i: t.transpose(
                tabps[:, 128 + 16 * i:128 + 16 * (i + 1)],
                metasb[0:16, 128 * i:128 * (i + 1)], idF[0:16, 0:16]),
                waits=[(VE, ve_meta)] if i == 0 else [])
        tt_meta = S.mark("tensor", TT)
        ve_metaT = S.op("vector", lambda v: v.tensor_copy(metaT_sb[:, 0:64], tabps[:, 128:192]),
                        waits=[(VE, S.val(VE)), (TT, tt_meta)], inc=(VE, 1))
        S.op("vector", lambda v: v.tensor_copy(FI[:, 0:16], metaT_sb[:, 0:16]),
             waits=[(VE, S.val(VE)), (VE, ve_metaT)], inc=(VE, 1))
        ve_fi = S.op("vector", lambda v: v.tensor_copy(FI[:, 16:32], metaT_sb[:, 16:32]),
                     inc=(VE, 1), waits=[(VE, S.val(VE))])
        if debug:
            S.op("sync", lambda q: q.dma_start(out=dbg["metaT"], in_=metaT_sb[:]),
                 waits=[(VE, ve_metaT)], inc=(DBG, 16))

        ps6.close()
        rt_scope.close()

        SCH = 256
        NSC = C // SCH
        ffn_scope = ExitStack()
        fsb = lambda name, shape, dt: ffn_scope.enter_context(nc.sbuf_tensor(name, shape, dt))
        wed_sb = fsb("wed_sb", [128, KF * D], BF16)
        hT = fsb("hT", [128, KF * SCH], BF16)
        dispT = [fsb(f"dispT{i}", [128, KD * SCH], BF16) for i in range(2)]
        gtok = [fsb(f"gtok{i}", [128, D], BF16) for i in range(2)]
        eoc = [fsb(f"eoc{i}", [128, D], F32) for i in range(2)]
        eg1 = [fsb(f"eg1{i}", [128, D], F32) for i in range(2)]
        eg2 = [fsb(f"eg2{i}", [128, D], F32) for i in range(2)]
        ocb = eoc

        S.op("gpsimd", lambda g: g.dma_start(
            out=wed_sb[:].rearrange("p (k n) -> p k n", k=KF),
            in_=wedd.rearrange("(k p) n -> p k n", p=128)),
            waits=[(TE, te_meta), (VE, ve_fi)], inc=(WED, 16))

        ps7 = ExitStack()
        hps = [ps7.enter_context(nc.psum_tensor(f"hps{i}", [128, 512], F32)) for i in range(4)]
        eops = [ps7.enter_context(nc.psum_tensor(f"eops{i}", [128, 512], F32)) for i in range(2)]

        ig_use = [0, 0]
        stg_use = [0, 0]
        for k in range(32):
            i = k % 2
            w = [(AG, ag_both), (VE, ve_fi)]
            if k >= 2:
                w.append((STG[i], 16 * stg_use[i]))
            ig_use[i] += 1
            S.op("gpsimd", lambda g, k=k, i=i: g.indirect_dma_start(
                gtok[i][:], None, agt_out, IndirectOffsetOnAxis(SIDX[:, k:k + 1], 0)),
                waits=w, inc=(IGS[i], 16))
            S.op("sync", lambda q, k=k, i=i: q.dma_start(
                out=dispd[k * 128:(k + 1) * 128, :], in_=gtok[i][:]),
                waits=[(IGS[i], 16 * ig_use[i])], inc=(STG[i], 16))
            stg_use[i] += 1

        dt_use = [0, 0]
        gelu_done = {}
        h_te = {}
        eo_te = {}
        eo_ve_prev = {}
        ste_latest = 0
        for s_ in range(NSC):
            i = s_ % 2
            dt_use[i] += 1
            for j in range(KD):
                w = [(STG[0], 16 * (s_ + 1)), (STG[1], 16 * (s_ + 1))]
                if s_ >= 2:
                    w.append((TE, h_te[s_ - 2]))
                S.op("sync", lambda q, s_=s_, i=i, j=j: q.dma_start(
                    out=dispT[i][:, j * SCH:(j + 1) * SCH],
                    in_=dispd[s_ * SCH:(s_ + 1) * SCH, j * 128:(j + 1) * 128], transpose=True),
                    waits=w if j == 0 else [], inc=(DT[i], 16))
            dispT_ld = (DT[i], 16 * KD * dt_use[i])
            for m in range(KF):
                w = [dispT_ld, (WEG, 16)]
                if s_ > 0:
                    w.append((TE, eo_te[s_ - 1]))
                if m >= 4:
                    w.append((SC, gelu_done[(s_, m - 4)]))
                elif s_ > 0:
                    w.append((SC, gelu_done[(s_ - 1, KF - 4 + m)]))
                for j in range(KD):
                    S.op("tensor", lambda t, s_=s_, i=i, m=m, j=j: t.matmul(
                        hps[m % 4][:, 0:SCH], weg_sb[:, j * FF + m * 128: j * FF + (m + 1) * 128],
                        dispT[i][:, j * SCH:(j + 1) * SCH],
                        start=(j == 0), stop=(j == KD - 1)), waits=w if j == 0 else [])
                te_h = S.mark("tensor", TE)
                gelu_done[(s_, m)] = S.op("scalar", lambda sc, s_=s_, m=m: sc.activation(
                    out=hT[:, m * SCH:(m + 1) * SCH], in_=hps[m % 4][:, 0:SCH], func=AF.Gelu),
                    waits=[(TE, te_h)], inc=(SC, 1))
            h_te[s_] = te_h
            for mt in range(2):
                for n in range(2):
                    w = [(SC, gelu_done[(s_, KF - 1)]), (WED, 16)]
                    if (s_, mt) != (0, 0):
                        w.append((VE, eo_ve_prev[n]))
                    for j in range(KF):
                        S.op("tensor", lambda t, mt=mt, n=n, j=j: t.matmul(
                            eops[n][:], hT[:, j * SCH + mt * 128: j * SCH + (mt + 1) * 128],
                            wed_sb[:, j * D + n * 512: j * D + (n + 1) * 512],
                            start=(j == 0), stop=(j == KF - 1)), waits=w if j == 0 else [])
                te_eo = S.mark("tensor", TE)
                for n in range(2):
                    wv = [(TE, te_eo)]
                    if ste_latest:
                        wv.append((STE, ste_latest))
                    eo_ve_prev[n] = S.op("vector", lambda v, mt=mt, n=n: v.tensor_copy(
                        eoc[mt % 2][:, n * 512:(n + 1) * 512], eops[n][:]), waits=wv, inc=(VE, 1))
                S.op("sync", lambda q, s_=s_, mt=mt: q.dma_start(
                    out=eo_din[s_ * SCH + mt * 128: s_ * SCH + (mt + 1) * 128, :],
                    in_=eoc[mt % 2][:]), waits=[(VE, eo_ve_prev[1])], inc=(STE, 16))
                if debug:
                    S.op("sync", lambda q, s_=s_, mt=mt: q.dma_start(
                        out=dbg["eo"][s_ * SCH + mt * 128: s_ * SCH + (mt + 1) * 128, :],
                        in_=eoc[mt % 2][:]), inc=(DBG, 16))
                ste_latest = S.val(STE)
            eo_te[s_] = te_eo

        ag_eo = S.op("gpsimd", lambda g: g.collective_compute(
            "AllGather", ALU.bypass, replica_groups=[list(range(NCORES))],
            ins=[eo_din], outs=[eo_dout]), waits=[(STE, S.val(STE))], inc=(AG, 1))
        ag_all = S.val(AG)

        eg1_use = [0, 0]
        eg2_use = [0, 0]
        sto_use = [0, 0]
        cmb_ve = {}
        for c in range(NCH):
            i = c % 2
            w = [(AG, ag_all), (VE, ve_fi)]
            if c >= 2:
                w.append((VE, cmb_ve[c - 2]))
            eg1_use[i] += 1
            S.op("gpsimd", lambda g, c=c, i=i: g.indirect_dma_start(
                eg1[i][:], None, eo_dout, IndirectOffsetOnAxis(FI[:, c:c + 1], 0)),
                waits=w, inc=(EG1S[i], 16))
            eg2_use[i] += 1
            S.op("gpsimd", lambda g, c=c, i=i: g.indirect_dma_start(
                eg2[i][:], None, eo_dout, IndirectOffsetOnAxis(FI[:, 16 + c:17 + c], 0)),
                inc=(EG2S[i], 16))
            wld = [(VE, xl2f_ve[NCH - 1])]
            if c >= 2:
                wld.append((VE, cmb_ve[c - 2]))
            x1_use[i] += 1
            S.op("sync", lambda q, c=c, i=i: q.dma_start(
                out=x1c[i][:], in_=x1d.rearrange("(c p) d -> p c d", p=128)[:, c, :]),
                waits=wld, inc=(X1S[i], 16))
            wcb = [(EG1S[i], 16 * eg1_use[i]), (EG2S[i], 16 * eg2_use[i]),
                   (X1S[i], 16 * x1_use[i])]
            if c < 2:
                wcb.append((STE, S.val(STE)))
            S.op("vector", lambda v, c=c, i=i: v.scalar_tensor_tensor(
                ocb[i][:], eg1[i][:], metaT_sb[:, 32 + c:33 + c], x1c[i][:],
                ALU.mult, ALU.add), waits=wcb)
            wcm = [(STO[i], 16 * sto_use[i])] if c >= 2 else []
            cmb_ve[c] = S.op("vector", lambda v, c=c, i=i: v.scalar_tensor_tensor(
                xl2f[i][:], eg2[i][:], metaT_sb[:, 48 + c:49 + c], ocb[i][:],
                ALU.mult, ALU.add), waits=wcm, inc=(VE, 1))
            sto_use[i] += 1
            S.op("sync", lambda q, c=c, i=i: q.dma_start(
                out=out_my.rearrange("(c p) d -> p c d", p=128)[:, c, :], in_=xl2f[i][:]),
                waits=[(VE, cmb_ve[c])], inc=(STO[i], 16))
        fw = [(STO[0], 16 * sto_use[0]), (STO[1], 16 * sto_use[1])]
        if debug:
            fw.append((DBG, S.val(DBG)))
        S.op("sync", lambda q: q.nop(), waits=fw)
        ps7.close()
        ffn_scope.close()
        ffw_scope.close()

        with nc.Block() as block:
            @block.sync
            def _(e):
                S.replay("sync", e)

            @block.gpsimd
            def _(e):
                S.replay("gpsimd", e)

            @block.scalar
            def _(e):
                S.replay("scalar", e)

            @block.vector
            def _(e):
                S.replay("vector", e)

            @block.tensor
            def _(e):
                S.replay("tensor", e)

    return nc


def _shards(inputs):
    x = np.ascontiguousarray(np.asarray(inputs["L0_attn_x"], dtype=np.float32)).reshape(T, D)
    wq = np.ascontiguousarray(np.asarray(inputs["L0_attn_Wq"], dtype=np.float32))
    wk = np.ascontiguousarray(np.asarray(inputs["L0_attn_Wk"], dtype=np.float32))
    wv = np.ascontiguousarray(np.asarray(inputs["L0_attn_Wv"], dtype=np.float32))
    wg = np.ascontiguousarray(np.asarray(inputs["L0_attn_Wg"], dtype=np.float32))
    wo = np.ascontiguousarray(np.asarray(inputs["L0_attn_Wo"], dtype=np.float32))
    bgv = np.ascontiguousarray(np.asarray(inputs["L0_attn_bg"], dtype=np.float32)).reshape(1, D)
    wgr = np.ascontiguousarray(np.asarray(inputs["L0_ffn_wg"], dtype=np.float32))
    wge = np.ascontiguousarray(np.asarray(inputs["L0_ffn_w_gate"], dtype=np.float32))
    wde = np.ascontiguousarray(np.asarray(inputs["L0_ffn_w_down"], dtype=np.float32))

    ident = np.eye(128, dtype=np.float32)
    tri = (np.arange(128)[:, None] < np.arange(128)[None, :]).astype(np.float32)

    maps = []
    for c in range(NCORES):
        sel = np.zeros((128, 16), dtype=np.float32)
        for j in range(16):
            sel[16 * c + j, j] = 1.0
        maps.append({
            "x_my": x[c * TLOC:(c + 1) * TLOC],
            "Wq": wq, "Wk": wk, "Wv": wv, "Wg": wg, "Wo": wo,
            "bgd": bgv, "wgrd": wgr,
            "wegd": np.ascontiguousarray(wge[c]),
            "wedd": np.ascontiguousarray(wde[c]),
            "identFd": ident, "identBd": ident, "triBd": tri,
            "sel16d": sel,
            "eidd": np.full((128, 1), float(c), dtype=np.float32),
        })
    return maps


_NC_CACHE = {}


def kernel(**inputs):
    key = ("v2", DEBUG)
    if key not in _NC_CACHE:
        _NC_CACHE[key] = build_nc(debug=DEBUG)
    nc = _NC_CACHE[key]
    maps = _shards(inputs)
    res = run_bass_kernel_spmd(nc, maps, core_ids=list(range(NCORES)))
    out = np.concatenate([res.results[c]["out_my"] for c in range(NCORES)], axis=0)
    if DEBUG:
        kernel._last = res
    return out.reshape(4, 4096, D)
